# version 30 (final)
# speedup vs baseline: 1.3611x; 1.1431x over previous
import contextlib
import numpy as np
import concourse.bass as bass
import concourse.mybir as mybir
from concourse.bass_utils import run_bass_kernel_spmd

F32 = mybir.dt.float32
AF = mybir.ActivationFunctionType
ALU = mybir.AluOpType
AXX = mybir.AxisListType.X

D = 1024
N = 4096
NH = 16
HD = 64
NOWN = 2048
INTER = 2048
NE = 8
CAP = 640
EPS = 1e-5

ENGS = ("SP", "PE", "DVE", "ACT", "GP")


class Buf:
    __slots__ = ("w", "r")
    def __init__(self):
        self.w = None
        self.r = []


class Plan:

    def __init__(self):
        self.cnt = {}
        self.marks = {e: {} for e in ENGS}
        self.snaps = {}
        self.pending_w = {e: [] for e in ENGS}
        self.pending_r = {e: [] for e in ENGS}
        self.seq = None
        self.phase = None

    def begin(self, phase):
        self.phase = phase
        self.seq = {e: [] for e in ENGS}

    def esem(self, e):
        return e + self.phase

    def _wait(self, e, sem, val):
        m = self.marks[e]
        if m.get(sem, 0) >= val:
            return
        self.seq[e].append(("w", sem, val))
        snap = self.snaps.get((sem, val))
        if snap:
            for k, v in snap.items():
                if m.get(k, 0) < v:
                    m[k] = v
        if m.get(sem, 0) < val:
            m[sem] = val

    def op(self, e, fn, reads=(), writes=(), inc=True, dma=False, q=""):
        sem = self.esem(e) + ("d_" + q if dma else "")
        amt = 16 if dma else 1
        need = {}

        def add(t):
            if t is not None and need.get(t[0], 0) < t[1]:
                need[t[0]] = t[1]

        for b in reads:
            add(b.w)
        for b in writes:
            add(b.w)
            for t in b.r:
                add(t)
        for sname in sorted(need):
            v = need[sname]
            assert v <= self.cnt.get(sname, 0), (
                f"wait target {sname}>={v} not yet planned (close groups first)")
            if "d_" in sname:
                assert v == self.cnt.get(sname, 0), (
                    f"unsafe intermediate wait {sname}>={v} (stream total "
                    f"{self.cnt.get(sname, 0)}): DMA completions are unordered")
            self._wait(e, sname, v)
        for b in writes:
            for e2 in ENGS:
                if e2 == e:
                    continue
                assert all(b is not x for x in self.pending_w[e2]), "write vs pending write"
                assert all(b is not x for x in self.pending_r[e2]), (
                    f"write to buf with pending read on {e2}: close the group first")
        for b in reads:
            for e2 in ENGS:
                if e2 == e:
                    continue
                assert all(b is not x for x in self.pending_w[e2]), (
                    f"read of buf with pending write on {e2}")
        self.seq[e].append(("op", fn, sem if inc else None, amt))
        if inc:
            v = self.cnt.get(sem, 0) + amt
            self.cnt[sem] = v
            tok = (sem, v)
            snap = dict(self.marks[e])
            snap[sem] = v
            self.snaps[(sem, v)] = snap
            for b in self.pending_w[e]:
                b.w = tok
                b.r = []
            self.pending_w[e].clear()
            for b in self.pending_r[e]:
                b.r.append(tok)
            self.pending_r[e].clear()
            for b in writes:
                b.w = tok
                b.r = []
            for b in reads:
                b.r.append(tok)
        else:
            for b in writes:
                self.pending_w[e].append(b)
            for b in reads:
                self.pending_r[e].append(b)

    def emit(self, nc, sems):
        assert not any(self.pending_w[e] for e in ENGS)
        for e in ENGS:
            pref = e + self.phase + "d_"
            for sem, c in sorted(self.cnt.items()):
                if sem.startswith(pref) and self.marks[e].get(sem, 0) < c:
                    self.seq[e].append(("w", sem, c))
                    self.marks[e][sem] = c
        seq = self.seq
        self.seq = None
        for e in ENGS:
            for item in seq[e]:
                sems[item[1] if item[0] == "w" else item[2]] if item[0] == "w" or item[2] else None

        def runner(e):
            def body(h):
                for item in seq[e]:
                    if item[0] == "w":
                        h.wait_ge(sems[item[1]], item[2])
                    else:
                        _, fn, sem, amt = item
                        inst = fn(h)
                        if sem is not None:
                            inst.then_inc(sems[sem], amt)
            return body

        with nc.Block() as blk:
            blk.sync(runner("SP"))
            blk.tensor(runner("PE"))
            blk.vector(runner("DVE"))
            blk.scalar(runner("ACT"))
            blk.gpsimd(runner("GP"))


def build_nc(upto="E"):
    nc = bass.Bass(trn_type="TRN2", target_bir_lowering=False)

    xb = nc.dram_tensor("xb", [N, D], F32, kind="ExternalInput")
    wq = nc.dram_tensor("wq", [D, D], F32, kind="ExternalInput")
    wk = nc.dram_tensor("wk", [D, D], F32, kind="ExternalInput")
    wv = nc.dram_tensor("wv", [D, D], F32, kind="ExternalInput")
    flagv_d = nc.dram_tensor("flagv", [128, 32], F32, kind="ExternalInput")
    gwt_d = nc.dram_tensor("gwt", [D, NE], F32, kind="ExternalInput")
    w13b = nc.dram_tensor("w13b", [NE, 32, 128, 1024], F32, kind="ExternalInput")
    w2b = nc.dram_tensor("w2b", [NE, 16, 128, 1024], F32, kind="ExternalInput")
    ident_d = nc.dram_tensor("ident", [128, 128], F32, kind="ExternalInput")
    tri_d = nc.dram_tensor("tri", [128, 128], F32, kind="ExternalInput")
    onesb_d = nc.dram_tensor("onesb", [8, 128, 16], F32, kind="ExternalInput")
    ones16_d = nc.dram_tensor("ones16", [16, 128], F32, kind="ExternalInput")
    diag16_d = nc.dram_tensor("diag16", [16, 16], F32, kind="ExternalInput")
    onescol_d = nc.dram_tensor("onescol", [128, 32], F32, kind="ExternalInput")
    ones1_d = nc.dram_tensor("ones1", [1, 128], F32, kind="ExternalInput")
    ecapm1_d = nc.dram_tensor("ecapm1", [128, 8], F32, kind="ExternalInput")
    dumb8_d = nc.dram_tensor("dumb8", [128, 8], F32, kind="ExternalInput")
    y = nc.dram_tensor("y", [NOWN, D], F32, kind="ExternalOutput")

    BFD = mybir.dt.bfloat16
    khiD = nc.dram_tensor("khiD", [D, N], BFD, kind="Internal")
    kloD = nc.dram_tensor("kloD", [D, N], BFD, kind="Internal")
    qhiD = nc.dram_tensor("qhiD", [NH, 64, NOWN], BFD, kind="Internal")
    qloD = nc.dram_tensor("qloD", [NH, 64, NOWN], BFD, kind="Internal")
    vtmd = nc.dram_tensor("vtmd", [N, D], F32, kind="Internal")
    attnd = nc.dram_tensor("attnd", [NOWN, D], F32, kind="Internal")
    ln2d = nc.dram_tensor("ln2d", [NOWN, D], F32, kind="Internal")
    x1d = nc.dram_tensor("x1d", [NOWN, D], F32, kind="Internal")
    xgd = nc.dram_tensor("xgd", [NE * CAP + 128 * 128, 1032], F32, kind="Internal")
    ygd = nc.dram_tensor("ygd", [NE * CAP, D], F32, kind="Internal")

    P = Plan()
    es = contextlib.ExitStack()
    with es:
        class SemDict(dict):
            def __missing__(self, name):
                h = es.enter_context(nc.semaphore("sem_" + name))
                self[name] = h
                return h
        sems = SemDict()

        ident = es.enter_context(nc.sbuf_tensor("s_ident", [128, 128], F32))
        tri = es.enter_context(nc.sbuf_tensor("s_tri", [128, 128], F32))
        onesb = [es.enter_context(nc.sbuf_tensor(f"s_onesb{c}", [128, 16], F32)) for c in range(8)]
        ones16 = es.enter_context(nc.sbuf_tensor("s_ones16", [16, 128], F32))
        diag16 = es.enter_context(nc.sbuf_tensor("s_diag16", [16, 16], F32))
        onescol = es.enter_context(nc.sbuf_tensor("s_onescol", [128, 32], F32))
        ones1 = es.enter_context(nc.sbuf_tensor("s_ones1", [1, 128], F32))
        ecapm1 = es.enter_context(nc.sbuf_tensor("s_ecapm1", [128, 8], F32))
        dumb8 = es.enter_context(nc.sbuf_tensor("s_dumb8", [128, 8], F32))
        flagv = es.enter_context(nc.sbuf_tensor("s_flagv", [128, 32], F32))
        gwt = [es.enter_context(nc.sbuf_tensor(f"s_gwt{c}", [128, NE], F32)) for c in range(8)]
        negcB = es.enter_context(nc.sbuf_tensor("negcB", [128, 16], F32))
        b_negcB = Buf()

        P.begin("0")
        consts = [(ident, ident_d), (tri, tri_d), (ones16, ones16_d), (diag16, diag16_d),
                  (onescol, onescol_d), (ones1, ones1_d), (ecapm1, ecapm1_d),
                  (dumb8, dumb8_d), (flagv, flagv_d)]
        b_const = Buf()
        for t, dsrc in consts[:-1]:
            P.op("SP", (lambda h, t=t, d=dsrc: h.dma_start(out=t[:, :], in_=d[:, :])), dma=True, q="c")
        for c in range(8):
            P.op("SP", (lambda h, c=c: h.dma_start(out=onesb[c][:, :], in_=onesb_d[c])), dma=True, q="c")
            P.op("SP", (lambda h, c=c: h.dma_start(out=gwt[c][:, :], in_=gwt_d[c * 128:(c + 1) * 128, :])), dma=True, q="c")
        P.op("SP", (lambda h: h.dma_start(out=flagv[:, :], in_=flagv_d[:, :])), writes=[b_const], dma=True, q="c")
        P.emit(nc, sems)

        b_xgd = [Buf(), Buf(), Buf()]
        esQ = es.enter_context(contextlib.ExitStack())
        maxk2 = esQ.enter_context(nc.sbuf_tensor("maxk2", [16, 1], F32))
        maxq2 = esQ.enter_context(nc.sbuf_tensor("maxq2", [16, 1], F32))
        b_maxk2, b_maxq2 = Buf(), Buf()

        with contextlib.ExitStack() as esA:
            BA = mybir.dt.bfloat16
            wtmp = [esA.enter_context(nc.sbuf_tensor(f"wtmp{i}", [128, D], F32)) for i in range(2)]
            whiA = [[esA.enter_context(nc.sbuf_tensor(f"whiA{m}_{c}", [128, D], BA)) for c in range(8)]
                    for m in range(3)]
            wloA = [[esA.enter_context(nc.sbuf_tensor(f"wloA{m}_{c}", [128, D], BA)) for c in range(8)]
                    for m in range(3)]
            xst = [esA.enter_context(nc.sbuf_tensor(f"xst{i}", [128, D], F32)) for i in range(4)]
            sqscr = esA.enter_context(nc.sbuf_tensor("sqscr", [128, D], F32))
            cols = esA.enter_context(nc.sbuf_tensor("cols", [128, 8], F32))
            lnhi = [[esA.enter_context(nc.sbuf_tensor(f"lnhi{p}_{c}", [128, 512], BA)) for c in range(8)]
                    for p in range(2)]
            lnlo = [[esA.enter_context(nc.sbuf_tensor(f"lnlo{p}_{c}", [128, 512], BA)) for c in range(8)]
                    for p in range(2)]
            kst = [esA.enter_context(nc.sbuf_tensor(f"kst{i}", [128, 512], BA)) for i in range(2)]
            klst = [esA.enter_context(nc.sbuf_tensor(f"klst{i}", [128, 512], BA)) for i in range(2)]
            vst = [esA.enter_context(nc.sbuf_tensor(f"vst{i}", [128, 512], F32)) for i in range(2)]
            sq2 = [esA.enter_context(nc.sbuf_tensor(f"sq2_{i}", [128, 512], F32)) for i in range(2)]
            nst = esA.enter_context(nc.sbuf_tensor("nst", [16, 512], F32))
            ncol = esA.enter_context(nc.sbuf_tensor("ncol", [16, 2], F32))
            pT = [esA.enter_context(nc.psum_tensor(f"pTa{i}", [128, 512], F32)) for i in range(2)]
            pK2 = [esA.enter_context(nc.psum_tensor(f"pK{i}", [128, 512], F32)) for i in range(2)]
            pV2 = [esA.enter_context(nc.psum_tensor(f"pV{i}", [128, 512], F32)) for i in range(2)]
            pN = esA.enter_context(nc.psum_tensor("pN", [16, 512], F32))
            pB = esA.enter_context(nc.psum_tensor("pB", [128, 16], F32))

            b_wtmp = [Buf(), Buf()]
            b_whiA = [[Buf() for _ in range(8)] for _ in range(3)]
            b_wloA = [[Buf() for _ in range(8)] for _ in range(3)]
            b_xst = [Buf() for _ in range(4)]
            b_sqscr, b_cols = Buf(), Buf()
            b_azt = b_sqscr
            b_lnhi = [[Buf() for _ in range(8)] for _ in range(2)]
            b_lnlo = [[Buf() for _ in range(8)] for _ in range(2)]
            b_kst = [Buf(), Buf()]
            b_klst = [Buf(), Buf()]
            b_vst = [Buf(), Buf()]
            b_sq2 = [Buf(), Buf()]
            b_nst, b_ncol = Buf(), Buf()
            b_pT = [Buf(), Buf()]
            b_pK2 = [Buf(), Buf()]
            b_pV2 = [Buf(), Buf()]
            b_pN, b_pB = Buf(), Buf()
            b_kfmd = [Buf(), Buf()]
            b_klod = [Buf(), Buf()]
            b_qhx = [Buf(), Buf()]
            b_qlx = [Buf(), Buf()]
            b_vtmd = [Buf(), Buf()]

            P.begin("A")
            for m, wsrc in enumerate((wq, wk, wv)):
                for c in range(8):
                    j = m * 8 + c
                    wt, bwt = wtmp[j % 2], b_wtmp[j % 2]
                    P.op("SP", (lambda h, c=c, wsrc=wsrc, wt=wt: h.dma_start(
                        out=wt[:, :], in_=wsrc[c * 128:(c + 1) * 128, :])),
                         writes=[bwt], dma=True, q=f"wl{j % 2}")
                    P.op("ACT", (lambda h, m=m, c=c, wt=wt: h.copy(whiA[m][c][:, :], wt[:, :])),
                         reads=[bwt], writes=[b_whiA[m][c]])
                    P.op("DVE", (lambda h, m=m, c=c, wt=wt: h.tensor_tensor(
                        wloA[m][c][:, :], wt[:, :], whiA[m][c][:, :], ALU.subtract)),
                         reads=[bwt, b_whiA[m][c]], writes=[b_wloA[m][c]])

            P.op("DVE", (lambda h: h.memset(sqscr[:, :], 0.0)), writes=[b_azt])
            for r in range(NE * CAP // 128):
                P.op("GP", (lambda h, r=r: h.dma_start(
                    out=xgd[r * 128:(r + 1) * 128, 0:1024], in_=sqscr[:, :])),
                     reads=[b_azt], writes=[b_xgd[0]], dma=True, q="zi")
                P.op("GP", (lambda h, r=r: h.dma_start(
                    out=xgd[r * 128:(r + 1) * 128, 1024:1032], in_=sqscr[:, 0:8])),
                     reads=[b_azt], writes=[b_xgd[0]], dma=True, q="zi")

            def ln_tile(xsb, b_xsb, row0, qx):
                P.op("SP", (lambda h: h.dma_start(out=xsb[:, :], in_=xb[row0:row0 + 128, :])),
                     writes=[b_xsb], dma=True, q=qx)
                P.op("DVE", (lambda h: h.reduce_sum(cols[:, 0:1], xsb[:, :], axis=AXX)),
                     reads=[b_xsb], writes=[b_cols])
                P.op("DVE", (lambda h: h.tensor_scalar(cols[:, 0:1], cols[:, 0:1], 1.0 / D, None, ALU.mult)),
                     reads=[b_cols], writes=[b_cols])
                P.op("DVE", (lambda h: h.tensor_scalar(xsb[:, :], xsb[:, :], cols[:, 0:1], None, ALU.subtract)),
                     reads=[b_xsb, b_cols], writes=[b_xsb])
                P.op("DVE", (lambda h: h.tensor_tensor(sqscr[:, :], xsb[:, :], xsb[:, :], ALU.mult)),
                     reads=[b_xsb], writes=[b_sqscr])
                P.op("DVE", (lambda h: h.reduce_sum(cols[:, 1:2], sqscr[:, :], axis=AXX)),
                     reads=[b_sqscr], writes=[b_cols])
                P.op("DVE", (lambda h: h.tensor_scalar(cols[:, 1:2], cols[:, 1:2], 1.0 / D, EPS, ALU.mult, ALU.add)),
                     reads=[b_cols], writes=[b_cols])
                P.op("ACT", (lambda h: h.activation(cols[:, 2:3], cols[:, 1:2], AF.Ln)),
                     reads=[b_cols], writes=[b_cols])
                P.op("ACT", (lambda h: h.activation(cols[:, 3:4], cols[:, 2:3], AF.Exp, scale=-0.5)),
                     reads=[b_cols], writes=[b_cols])
                P.op("DVE", (lambda h: h.tensor_scalar(xsb[:, :], xsb[:, :], cols[:, 3:4], None, ALU.mult)),
                     reads=[b_xsb, b_cols], writes=[b_xsb])

            def transpose_to_lnfm(xsb, b_xsb, tt, par):
                for cg in range(2):
                    pt, bpt = pT[cg], b_pT[cg]
                    for c4 in range(4):
                        c = cg * 4 + c4
                        P.op("PE", (lambda h, c=c, c4=c4, pt=pt, xsb=xsb: h.transpose(
                            pt[:, c4 * 128:(c4 + 1) * 128],
                            xsb[:, c * 128:(c + 1) * 128], ident[:, :])),
                             reads=[b_xsb, b_const], writes=[bpt])
                    for c4 in range(4):
                        c = cg * 4 + c4
                        P.op("ACT", (lambda h, c=c, c4=c4, tt=tt, pt=pt, par=par: h.copy(
                            lnhi[par][c][:, tt * 128:(tt + 1) * 128], pt[:, c4 * 128:(c4 + 1) * 128])),
                             reads=[bpt], writes=[b_lnhi[par][c]], inc=(c4 == 3))
                    for c4 in range(4):
                        c = cg * 4 + c4
                        P.op("DVE", (lambda h, c=c, c4=c4, tt=tt, pt=pt, par=par: h.tensor_tensor(
                            lnlo[par][c][:, tt * 128:(tt + 1) * 128],
                            pt[:, c4 * 128:(c4 + 1) * 128],
                            lnhi[par][c][:, tt * 128:(tt + 1) * 128], ALU.subtract)),
                             reads=[bpt, b_lnhi[par][c]], writes=[b_lnlo[par][c]])

            first_k = True
            first_q = True

            def emit_ln(g, tt):
                ln_tile(xst[tt], b_xst[tt], (4 * g + tt) * 128, f"x{tt}")

            for tt in range(4):
                emit_ln(0, tt)
            for g in range(8):
                par = g % 2
                for tt in range(4):
                    transpose_to_lnfm(xst[tt], b_xst[tt], tt, par)

                def knorm_mm(dq):
                    P.op("PE", (lambda h, dq=dq: h.matmul(pN[:, :], onesb[dq][:, :], sq2[dq % 2][:, :],
                                                          start=(dq == 0), stop=(dq == 7))),
                         reads=[b_sq2[dq % 2], b_const], writes=[b_pN], inc=(dq == 7))

                def proj_mms(m, dq, pK, b_pK):
                    nmm = 0
                    for wmat, bwmat, lnt, blnt in ((whiA[m], b_whiA[m], lnhi[par], b_lnhi[par]),
                                                   (whiA[m], b_whiA[m], lnlo[par], b_lnlo[par]),
                                                   (wloA[m], b_wloA[m], lnhi[par], b_lnhi[par])):
                        for hc in range(8):
                            nmm += 1
                            P.op("PE", (lambda h, dq=dq, hc=hc, pK=pK, wmat=wmat, lnt=lnt,
                                        st=(nmm == 1), sp=(nmm == 24): h.matmul(
                                pK[:, :], wmat[hc][:, dq * 128:(dq + 1) * 128], lnt[hc][:, :],
                                start=st, stop=sp)),
                                 reads=[bwmat[hc], blnt[hc]], writes=[b_pK], inc=(nmm == 24))

                def split_evac(pK, b_pK, dq):
                    ks, bks = kst[dq % 2], b_kst[dq % 2]
                    kl, bkl = klst[dq % 2], b_klst[dq % 2]
                    P.op("ACT", (lambda h, ks=ks, pK=pK: h.copy(ks[:, :], pK[:, :])),
                         reads=[b_pK], writes=[bks])
                    P.op("DVE", (lambda h, ks=ks, kl=kl, pK=pK: h.tensor_tensor(
                        kl[:, :], pK[:, :], ks[:, :], ALU.subtract)),
                         reads=[b_pK, bks], writes=[bkl])
                    P.op("DVE", (lambda h, ks=ks, dq=dq: h.tensor_tensor(
                        sq2[dq % 2][:, :], ks[:, :], ks[:, :], ALU.mult)),
                         reads=[bks], writes=[b_sq2[dq % 2]])
                    return ks, bks, kl, bkl

                for dq in range(8):
                    pK, b_pK = pK2[dq % 2], b_pK2[dq % 2]
                    proj_mms(1, dq, pK, b_pK)
                    if dq > 0:
                        knorm_mm(dq - 1)
                    ks, bks, kl, bkl = split_evac(pK, b_pK, dq)
                    P.op("GP", (lambda h, ks=ks, dq=dq, g=g: h.dma_start(
                        out=khiD[dq * 128:(dq + 1) * 128, g * 512:(g + 1) * 512], in_=ks[:, :])),
                         reads=[bks], writes=[b_kfmd[dq % 2]], dma=True, q=f"ks{dq % 2}h")
                    P.op("GP", (lambda h, kl=kl, dq=dq, g=g: h.dma_start(
                        out=kloD[dq * 128:(dq + 1) * 128, g * 512:(g + 1) * 512], in_=kl[:, :])),
                         reads=[bkl], writes=[b_klod[dq % 2]], dma=True, q=f"ks{dq % 2}l")
                    if dq % 2 == 1 and g < 7:
                        emit_ln(g + 1, dq // 2)
                knorm_mm(7)
                P.op("DVE", (lambda h: h.tensor_copy(nst[:, :], pN[:, :])), reads=[b_pN], writes=[b_nst])
                P.op("DVE", (lambda h: h.reduce_max(ncol[:, 0:1], nst[:, :], axis=AXX)),
                     reads=[b_nst], writes=[b_ncol])
                if first_k:
                    P.op("DVE", (lambda h: h.tensor_copy(maxk2[:, :], ncol[:, 0:1])),
                         reads=[b_ncol], writes=[b_maxk2])
                    first_k = False
                else:
                    P.op("DVE", (lambda h: h.tensor_tensor(maxk2[:, :], maxk2[:, :], ncol[:, 0:1], ALU.max)),
                         reads=[b_ncol, b_maxk2], writes=[b_maxk2])

                for tt in range(4):
                    t = 4 * g + tt
                    for dv in range(2):
                        pV, b_pV = pV2[dv], b_pV2[dv]
                        nmm = 0
                        for wmat, bwmat, lnt, blnt in ((whiA[2], b_whiA[2], lnhi[par], b_lnhi[par]),
                                                       (whiA[2], b_whiA[2], lnlo[par], b_lnlo[par]),
                                                       (wloA[2], b_wloA[2], lnhi[par], b_lnhi[par])):
                            for hc in range(8):
                                nmm += 1
                                P.op("PE", (lambda h, tt=tt, dv=dv, hc=hc, pV=pV, wmat=wmat, lnt=lnt,
                                            st=(nmm == 1), sp=(nmm == 24): h.matmul(
                                    pV[:, :], lnt[hc][:, tt * 128:(tt + 1) * 128],
                                    wmat[hc][:, dv * 512:(dv + 1) * 512],
                                    start=st, stop=sp)),
                                     reads=[bwmat[hc], blnt[hc]], writes=[b_pV], inc=(nmm == 24))
                        vs, bvs = vst[dv % 2], b_vst[dv % 2]
                        P.op("DVE", (lambda h, vs=vs, t=t, pV=pV: h.tensor_scalar(
                            vs[:, :], pV[:, :], flagv[:, t:t + 1], None, ALU.mult)),
                             reads=[b_pV, b_const], writes=[bvs])
                        P.op("GP", (lambda h, vs=vs, t=t, dv=dv: h.dma_start(
                            out=vtmd[t * 128:(t + 1) * 128, dv * 512:(dv + 1) * 512], in_=vs[:, :])),
                             reads=[bvs], writes=[b_vtmd[dv % 2]], dma=True, q=f"vs{dv % 2}")

                if g >= 4:
                    for dq in range(8):
                        pK, b_pK = pK2[dq % 2], b_pK2[dq % 2]
                        proj_mms(0, dq, pK, b_pK)
                        if dq > 0:
                            knorm_mm(dq - 1)
                        ks, bks, kl, bkl = split_evac(pK, b_pK, dq)
                        for hh in range(2):
                            P.op("GP", (lambda h, ks=ks, dq=dq, g=g, hh=hh: h.dma_start(
                                out=bass.AP(qhiD, (2 * dq + hh) * 64 * NOWN + (g - 4) * 512,
                                            [[NOWN, 64], [1, 512]]),
                                in_=ks[64 * hh:64 * hh + 64, :])),
                                 reads=[bks], writes=[b_qhx[dq % 2]], dma=True, q=f"qs{dq % 2}h")
                            P.op("GP", (lambda h, kl=kl, dq=dq, g=g, hh=hh: h.dma_start(
                                out=bass.AP(qloD, (2 * dq + hh) * 64 * NOWN + (g - 4) * 512,
                                            [[NOWN, 64], [1, 512]]),
                                in_=kl[64 * hh:64 * hh + 64, :])),
                                 reads=[bkl], writes=[b_qlx[dq % 2]], dma=True, q=f"qs{dq % 2}l")
                    knorm_mm(7)
                    P.op("DVE", (lambda h: h.tensor_copy(nst[:, :], pN[:, :])), reads=[b_pN], writes=[b_nst])
                    P.op("DVE", (lambda h: h.reduce_max(ncol[:, 1:2], nst[:, :], axis=AXX)),
                         reads=[b_nst], writes=[b_ncol])
                    if first_q:
                        P.op("DVE", (lambda h: h.tensor_copy(maxq2[:, :], ncol[:, 1:2])),
                             reads=[b_ncol], writes=[b_maxq2])
                        first_q = False
                    else:
                        P.op("DVE", (lambda h: h.tensor_tensor(maxq2[:, :], maxq2[:, :], ncol[:, 1:2], ALU.max)),
                             reads=[b_ncol, b_maxq2], writes=[b_maxq2])

            P.op("DVE", (lambda h: h.tensor_tensor(ncol[:, 0:1], maxq2[:, :], maxk2[:, :], ALU.mult)),
                 reads=[b_maxq2, b_maxk2], writes=[b_ncol])
            P.op("ACT", (lambda h: h.activation(ncol[:, 1:2], ncol[:, 0:1], AF.Ln)),
                 reads=[b_ncol], writes=[b_ncol])
            P.op("ACT", (lambda h: h.activation(ncol[:, 0:1], ncol[:, 1:2], AF.Exp, scale=0.5)),
                 reads=[b_ncol], writes=[b_ncol])
            P.op("DVE", (lambda h: h.tensor_scalar(nst[:, 0:16], diag16[:, :], ncol[:, 0:1], None, ALU.mult)),
                 reads=[b_ncol, b_const], writes=[b_nst])
            P.op("PE", (lambda h: h.matmul(pB[:, :], ones16[:, :], nst[:, 0:16], start=True, stop=True)),
                 reads=[b_nst, b_const], writes=[b_pB])
            P.op("DVE", (lambda h: h.tensor_scalar(negcB[:, :], pB[:, :], -0.125, None, ALU.mult)),
                 reads=[b_pB], writes=[b_negcB])
            P.emit(nc, sems)

        if upto == "A":
            return nc

        with contextlib.ExitStack() as esB:
            BB = mybir.dt.bfloat16
            kpa = [esB.enter_context(nc.sbuf_tensor(f"kpa{i}", [128, N], BB)) for i in range(2)]
            kpb = [esB.enter_context(nc.sbuf_tensor(f"kpb{i}", [128, N], BB)) for i in range(2)]
            qhid = [esB.enter_context(nc.sbuf_tensor(f"qhid{hd}", [128, NOWN], BB)) for hd in range(16)]
            qlop = [esB.enter_context(nc.sbuf_tensor(f"qlop{p}", [128, NOWN], BB)) for p in range(8)]
            vp = [esB.enter_context(nc.sbuf_tensor(f"vp{i}", [128, 32 * 130], F32)) for i in range(2)]
            pt_ = [[esB.enter_context(nc.sbuf_tensor(f"pt{h2}_{i}", [128, 512], F32)) for i in range(2)]
                   for h2 in range(2)]
            ptm = [esB.enter_context(nc.sbuf_tensor(f"ptm{h2}", [128, 128], F32)) for h2 in range(2)]
            otsb = [esB.enter_context(nc.sbuf_tensor(f"otsb{h2}", [65, 512], F32)) for h2 in range(2)]
            zacc = [esB.enter_context(nc.sbuf_tensor(f"zacc{h2}", [128, 512], F32)) for h2 in range(2)]
            zr = esB.enter_context(nc.sbuf_tensor("zr", [128, 4], F32))
            astage = [esB.enter_context(nc.sbuf_tensor(f"astage{i}", [128, 512], F32)) for i in range(2)]
            pS = [[esB.enter_context(nc.psum_tensor(f"pS{h2}_{i}", [128, 512], F32)) for i in range(2)]
                  for h2 in range(2)]
            pOx = [esB.enter_context(nc.psum_tensor(f"pO{h2}", [128, 1024], F32)) for h2 in range(2)]

            b_kpa = [Buf(), Buf()]
            b_kpb = [Buf(), Buf()]
            b_qearly, b_qlate = Buf(), Buf()
            b_vp = [Buf(), Buf()]
            b_pt = [[Buf(), Buf()], [Buf(), Buf()]]
            b_ptm = [Buf(), Buf()]
            b_otsb, b_zr = [Buf(), Buf()], Buf()
            b_zacc = [Buf(), Buf()]
            b_astage = [Buf(), Buf()]
            b_pS = [[Buf(), Buf()], [Buf(), Buf()]]
            b_pO = [Buf(), Buf()]
            b_attnd = [Buf(), Buf()]

            P.begin("B")

            def load_q(early):
                for head in range(16):
                    if (head < 4) != early:
                        continue
                    tok = b_qearly if head < 4 else b_qlate
                    qn = "qe" if head < 4 else "qlt"
                    for half in range(2):
                        P.op("SP", (lambda h, head=head, half=half: h.dma_start(
                            out=qhid[head][64 * half:64 * half + 64, :],
                            in_=bass.AP(qhiD, head * 64 * NOWN, [[NOWN, 64], [1, NOWN]]))),
                             reads=b_qhx + b_qlx, writes=[tok], dma=True, q=qn)
                for p in range(8):
                    if (p < 2) != early:
                        continue
                    tok = b_qearly if p < 2 else b_qlate
                    qn = "qe" if p < 2 else "qlt"
                    for half in range(2):
                        P.op("SP", (lambda h, p=p, half=half: h.dma_start(
                            out=qlop[p][64 * half:64 * half + 64, :],
                            in_=bass.AP(qloD, (2 * p + half) * 64 * NOWN, [[NOWN, 64], [1, NOWN]]))),
                             reads=b_qhx + b_qlx, writes=[tok], dma=True, q=qn)

            def load_pair(pair, slot):
                P.op("SP", (lambda h, pair=pair, slot=slot: h.dma_start(
                    out=kpa[slot][0:64, :], in_=khiD[pair * 128:pair * 128 + 64, :])),
                     reads=b_kfmd + b_klod, writes=[b_kpa[slot]], dma=True, q=f"kpa{slot}")
                P.op("SP", (lambda h, pair=pair, slot=slot: h.dma_start(
                    out=kpa[slot][64:128, :], in_=kloD[pair * 128:pair * 128 + 64, :])),
                     reads=b_kfmd + b_klod, writes=[b_kpa[slot]], dma=True, q=f"kpa{slot}")
                P.op("SP", (lambda h, pair=pair, slot=slot: h.dma_start(
                    out=kpb[slot][0:64, :], in_=kloD[pair * 128 + 64:pair * 128 + 128, :])),
                     reads=b_kfmd + b_klod, writes=[b_kpb[slot]], dma=True, q=f"kpb{slot}")
                P.op("SP", (lambda h, pair=pair, slot=slot: h.dma_start(
                    out=kpb[slot][64:128, :], in_=khiD[pair * 128 + 64:pair * 128 + 128, :])),
                     reads=b_kfmd + b_klod, writes=[b_kpb[slot]], dma=True, q=f"kpb{slot}")
                for h2 in range(2):
                    P.op("SP", (lambda h, pair=pair, slot=slot, h2=h2: h.dma_start(
                        out=bass.AP(vp[slot], 65 * h2, [[32 * 130, 128], [130, 32], [1, 64]]),
                        in_=bass.AP(vtmd, pair * 128 + 64 * h2,
                                    [[1024, 128], [128 * 1024, 32], [1, 64]]))),
                         reads=b_vtmd, writes=[b_vp[slot]], dma=True, q=f"vp{slot}")
                P.op("DVE", (lambda h, slot=slot: h.memset(vp[slot][:, 64:32 * 130:65], 1.0)),
                     reads=[b_vp[slot]], writes=[b_vp[slot]])

            load_pair(0, 0)
            load_q(True)
            load_q(False)
            for pair in range(8):
                slot = pair % 2
                if pair < 7:
                    load_pair(pair + 1, 1 - slot)
                for qq in range(4):
                    q0 = qq * 512

                    def emit_sexp(kc):
                        bq = b_qearly if pair < 2 else b_qlate
                        for h2 in range(2):
                            kpx = kpa[slot] if h2 == 0 else kpb[slot]
                            bkpx = b_kpa[slot] if h2 == 0 else b_kpb[slot]
                            P.op("PE", (lambda h, h2=h2, kc=kc, q0=q0, head=2 * pair + h2,
                                        kpx=kpx: h.matmul(
                                pS[h2][kc % 2][:, :],
                                kpx[:, kc * 128:(kc + 1) * 128],
                                qhid[head][:, q0:q0 + 512],
                                start=True, stop=False)),
                                 reads=[bkpx, bq], writes=[b_pS[h2][kc % 2]])
                        for h2 in range(2):
                            kpx = kpa[slot] if h2 == 0 else kpb[slot]
                            bkpx = b_kpa[slot] if h2 == 0 else b_kpb[slot]
                            P.op("PE", (lambda h, h2=h2, kc=kc, q0=q0, kpx=kpx, pair=pair: h.matmul(
                                pS[h2][kc % 2][:, :],
                                kpx[64 * h2:64 * h2 + 64, kc * 128:(kc + 1) * 128],
                                qlop[pair][64 * h2:64 * h2 + 64, q0:q0 + 512],
                                start=False, stop=True, tile_position=(64 * h2, 0))),
                                 reads=[bkpx, bq], writes=[b_pS[h2][kc % 2]], inc=True)
                        for h2 in range(2):
                            PT = pt_[h2][kc % 2]
                            bPT = b_pt[h2][kc % 2]
                            P.op("ACT", (lambda h, h2=h2, kc=kc, PT=PT, head=2 * pair + h2: h.activation(
                                PT[:, :], pS[h2][kc % 2][:, :],
                                AF.Exp, bias=negcB[:, head:head + 1], scale=0.125)),
                                 reads=[b_pS[h2][kc % 2], b_negcB], writes=[bPT])

                    def emit_mask(kc):
                        bstar = kc - 16 - 4 * qq
                        if 0 <= bstar < 4:
                            for h2 in range(2):
                                PT = pt_[h2][kc % 2]
                                bPT = b_pt[h2][kc % 2]
                                P.op("DVE", (lambda h, PT=PT, bstar=bstar, h2=h2: h.tensor_tensor(
                                    ptm[h2][:, :], PT[:, bstar * 128:(bstar + 1) * 128],
                                    tri[:, :], ALU.mult)),
                                     reads=[bPT, b_const], writes=[b_ptm[h2]])

                    def emit_vmm(kc):
                        gb = kc - 16
                        st = (kc == 0)
                        bstar = gb - 4 * qq
                        for h2 in range(2):
                            PT = pt_[h2][kc % 2]
                            bPT = b_pt[h2][kc % 2]
                            vb = kc * 130 + 65 * h2
                            if kc == 0:
                                P.op("DVE", (lambda h, h2=h2: h.memset(zacc[h2][:, :], 0.0)),
                                     reads=[b_zacc[h2]], writes=[b_zacc[h2]])

                            def zadd(chi, PT=PT, bPT=bPT, h2=h2):
                                P.op("DVE", (lambda h, chi=chi, PT=PT, h2=h2: h.scalar_tensor_tensor(
                                    zacc[h2][:, 0:chi], PT[:, 0:chi], 1.0, zacc[h2][:, 0:chi],
                                    ALU.mult, ALU.add)),
                                     reads=[bPT, b_zacc[h2]], writes=[b_zacc[h2]])

                            mms = []
                            if gb < 4 * qq:
                                mms.append((0, 65, 65, 0, 512, (0, 0), "pt"))
                            elif bstar < 4:
                                if kc == 31:
                                    if bstar > 0:
                                        mms.append((64, 65, 1, 0, bstar * 128, (0, 64), "pt"))
                                    mms.append((64, 65, 1, bstar * 128, (bstar + 1) * 128, (0, 64), "pt"))
                                else:
                                    zadd((bstar + 1) * 128)
                                mms.append((0, 64, 64, bstar * 128, (bstar + 1) * 128, (0, 0), "ptm"))
                                if bstar < 3:
                                    mms.append((0, 65, 65, (bstar + 1) * 128, 512, (0, 0), "pt"))
                            else:
                                if kc == 31:
                                    mms.append((64, 65, 1, 0, 512, (0, 64), "pt"))
                                else:
                                    zadd(512)
                            for i, (plo, phi, vw, clo, chi, tp, mv) in enumerate(mms):
                                last = (i == len(mms) - 1)
                                vcol0 = vb if plo == 0 and vw > 1 else vb + 64
                                vcol1 = vcol0 + vw
                                if mv == "ptm":
                                    P.op("PE", (lambda h, h2=h2, vcol0=vcol0, vcol1=vcol1, clo=clo, chi=chi,
                                                tp=tp, st=st, slot=slot: h.matmul(
                                        pOx[h2][0:64, clo:chi], vp[slot][:, vcol0:vcol1],
                                        ptm[h2][:, :], start=st, stop=False,
                                        tile_position=tp, skip_group_check=True)),
                                         reads=[b_ptm[h2], b_vp[slot]], writes=[b_pO[h2]], inc=last)
                                else:
                                    P.op("PE", (lambda h, h2=h2, plo=plo, phi=phi, vcol0=vcol0, vcol1=vcol1,
                                                clo=clo, chi=chi, tp=tp, st=st, PT=PT,
                                                slot=slot: h.matmul(
                                        pOx[h2][plo:phi, clo:chi], vp[slot][:, vcol0:vcol1],
                                        PT[:, clo:chi], start=st, stop=False,
                                        tile_position=tp, skip_group_check=True)),
                                         reads=[bPT, b_vp[slot]], writes=[b_pO[h2]], inc=last)
                            if kc == 31:
                                P.op("PE", (lambda h, h2=h2: h.matmul(
                                    pOx[h2][64:65, 0:512], onescol[:, 0:1], zacc[h2][:, :],
                                    start=False, stop=True, tile_position=(0, 64),
                                    skip_group_check=True)),
                                     reads=[b_zacc[h2], b_const], writes=[b_pO[h2]], inc=True)

                    emit_sexp(0)
                    emit_mask(0)
                    for kc in range(1, 32):
                        emit_sexp(kc)
                        emit_vmm(kc - 1)
                        emit_mask(kc)
                    emit_vmm(31)
                    sl = (4 * pair + qq) % 2
                    for h2 in range(2):
                        P.op("DVE", (lambda h, h2=h2: h.tensor_copy(
                            otsb[h2][:, 0:512], pOx[h2][0:65, 0:512])),
                             reads=[b_pO[h2]], writes=[b_otsb[h2]])
                        for blk in range(4):
                            P.op("PE", (lambda h, blk=blk, h2=h2: h.transpose(
                                pOx[h2][:, blk * 128:blk * 128 + 65],
                                otsb[h2][0:65, blk * 128:(blk + 1) * 128], ident[0:65, 0:65])),
                                 reads=[b_otsb[h2], b_const], writes=[b_pO[h2]], inc=(blk == 3))
                        for blk in range(4):
                            P.op("DVE", (lambda h, blk=blk, h2=h2: h.reciprocal(
                                zr[:, blk:blk + 1], pOx[h2][:, blk * 128 + 64:blk * 128 + 65])),
                                 reads=[b_pO[h2]], writes=[b_zr], inc=(blk == 3))
                        for blk in range(4):
                            P.op("DVE", (lambda h, h2=h2, blk=blk, sl=sl: h.tensor_scalar(
                                astage[sl][:, blk * 128 + 64 * h2:blk * 128 + 64 * h2 + 64],
                                pOx[h2][:, blk * 128:blk * 128 + 64],
                                zr[:, blk:blk + 1], None, ALU.mult)),
                                 reads=[b_pO[h2], b_zr], writes=[b_astage[sl]], inc=(blk == 3))
                    P.op("GP", (lambda h, pair=pair, qq=qq, sl=sl: h.dma_start(
                        out=bass.AP(attnd, qq * 512 * 1024 + pair * 128,
                                    [[1024, 128], [128 * 1024, 4], [1, 128]]),
                        in_=astage[sl][:, 0:512])),
                         reads=[b_astage[sl]], writes=[b_attnd[sl]], dma=True, q=f"at{sl}")
            P.emit(nc, sems)

        if upto == "B":
            return nc
        esQ.close()

        idx2all = es.enter_context(nc.sbuf_tensor("idx2all", [128, 32], mybir.dt.uint32))
        esL = es.enter_context(contextlib.ExitStack())
        ln2fm = [esL.enter_context(nc.sbuf_tensor(f"ln2fm{c}", [128, NOWN], F32)) for c in range(8)]
        combT = esL.enter_context(nc.sbuf_tensor("combT", [8, NOWN], F32))
        b_ln2fm = [Buf() for _ in range(8)]
        b_combT = Buf()
        b_idx2all = Buf()
        b_ln2d = [Buf(), Buf()]
        b_x1d = [Buf(), Buf()]
        b_ygd = [Buf(), Buf()]

        with contextlib.ExitStack() as esC:
            xst = [esC.enter_context(nc.sbuf_tensor(f"cxst{i}", [128, D], F32)) for i in range(2)]
            ast = [esC.enter_context(nc.sbuf_tensor(f"cast{i}", [128, D], F32)) for i in range(2)]
            ln2t = [esC.enter_context(nc.sbuf_tensor(f"ln2t{i}", [128, D], F32)) for i in range(2)]
            sqscr = esC.enter_context(nc.sbuf_tensor("csq", [128, D], F32))
            cols = esC.enter_context(nc.sbuf_tensor("ccols", [128, 8], F32))
            lgsb = esC.enter_context(nc.sbuf_tensor("lgsb", [8, NOWN], F32))
            ltok = esC.enter_context(nc.sbuf_tensor("ltok", [128, 8], F32))
            gsc = esC.enter_context(nc.sbuf_tensor("gsc", [128, 8], F32))
            gmsk = esC.enter_context(nc.sbuf_tensor("gmsk", [128, 8], F32))
            gtmp = esC.enter_context(nc.sbuf_tensor("gtmp", [128, 8], F32))
            gcol = esC.enter_context(nc.sbuf_tensor("gcol", [128, 8], F32))
            ctok = esC.enter_context(nc.sbuf_tensor("ctok", [128, 8], F32))
            ctokall = esC.enter_context(nc.sbuf_tensor("ctokall", [128, 128], F32))
            ltokall = esC.enter_context(nc.sbuf_tensor("ltokall", [128, 128], F32))
            gexp = esC.enter_context(nc.sbuf_tensor("gexp", [128, 128], F32))
            pm64 = esC.enter_context(nc.sbuf_tensor("pm64", [128, 64], F32))
            q32a = esC.enter_context(nc.sbuf_tensor("q32a", [128, 32], F32))
            q32b = esC.enter_context(nc.sbuf_tensor("q32b", [128, 32], F32))
            w16 = esC.enter_context(nc.sbuf_tensor("w16", [128, 64], F32))
            selm = esC.enter_context(nc.sbuf_tensor("selm", [128, 128], F32))
            rtokall = esC.enter_context(nc.sbuf_tensor("rtokall", [128, 128], F32))
            ones128 = esC.enter_context(nc.sbuf_tensor("ones128", [128, 128], F32))
            rka = esC.enter_context(nc.sbuf_tensor("rka", [8, NOWN], F32))
            rkb = esC.enter_context(nc.sbuf_tensor("rkb", [8, NOWN], F32))
            rtok = esC.enter_context(nc.sbuf_tensor("rtok", [128, 8], F32))
            aug = [esC.enter_context(nc.sbuf_tensor(f"caug{i}", [128, 1032], F32)) for i in range(2)]
            pT = [esC.enter_context(nc.psum_tensor(f"cpT{i}", [128, 512], F32)) for i in range(2)]
            pL = esC.enter_context(nc.psum_tensor("cpL", [8, 512], F32))
            pX = esC.enter_context(nc.psum_tensor("cpX", [128, 8], F32))
            pX2 = esC.enter_context(nc.psum_tensor("cpX2", [8, 128], F32))
            pX3 = esC.enter_context(nc.psum_tensor("cpX3", [128, 8], F32))

            b_xst = [Buf() for _ in range(4)]
            b_ast = [Buf(), Buf()]
            b_ln2t = [Buf(), Buf()]
            b_sq, b_cols = Buf(), Buf()
            b_lgsb, b_ltok = Buf(), Buf()
            b_gsc, b_gmsk, b_gtmp, b_gcol, b_ctok = Buf(), Buf(), Buf(), Buf(), Buf()
            b_ctokall = Buf()
            b_ltokall, b_gexp = Buf(), Buf()
            b_pm64, b_q32a, b_q32b, b_w16 = Buf(), Buf(), Buf(), Buf()
            b_selm, b_rtokall = Buf(), Buf()
            b_ones128, b_pX3 = Buf(), Buf()
            b_rka, b_rkb, b_rtok = Buf(), Buf(), Buf()
            b_aug = [Buf(), Buf()]
            b_pT = [Buf(), Buf()]
            b_pL, b_pX = Buf(), Buf()

            P.begin("C")
            for t in range(16):
                xsb, b_xsb = xst[t % 2], b_xst[t % 2]
                asb, b_asb = ast[t % 2], b_ast[t % 2]
                l2, b_l2 = ln2t[t % 2], b_ln2t[t % 2]
                P.op("SP", (lambda h, xsb=xsb, t=t: h.dma_start(
                    out=xsb[:, :], in_=xb[NOWN + t * 128:NOWN + (t + 1) * 128, :])),
                     writes=[b_xsb], dma=True, q=f"cx{t % 2}")
                P.op("SP", (lambda h, asb=asb, t=t: h.dma_start(
                    out=asb[:, :], in_=attnd[t * 128:(t + 1) * 128, :])),
                     reads=b_attnd, writes=[b_asb], dma=True, q=f"ca{t % 2}")
                P.op("DVE", (lambda h, xsb=xsb, asb=asb: h.tensor_tensor(xsb[:, :], xsb[:, :], asb[:, :], ALU.add)),
                     reads=[b_xsb, b_asb], writes=[b_xsb])
                P.op("GP", (lambda h, xsb=xsb, t=t: h.dma_start(
                    out=x1d[t * 128:(t + 1) * 128, :], in_=xsb[:, :])),
                     reads=[b_xsb], writes=[b_x1d[t % 2]], dma=True, q=f"x1{t % 2}")
                P.op("DVE", (lambda h, xsb=xsb: h.reduce_sum(cols[:, 0:1], xsb[:, :], axis=AXX)),
                     reads=[b_xsb], writes=[b_cols])
                P.op("DVE", (lambda h: h.tensor_scalar(cols[:, 0:1], cols[:, 0:1], 1.0 / D, None, ALU.mult)),
                     reads=[b_cols], writes=[b_cols])
                P.op("DVE", (lambda h, xsb=xsb, l2=l2: h.tensor_scalar(l2[:, :], xsb[:, :], cols[:, 0:1], None, ALU.subtract)),
                     reads=[b_xsb, b_cols], writes=[b_l2])
                P.op("DVE", (lambda h, l2=l2: h.tensor_tensor(sqscr[:, :], l2[:, :], l2[:, :], ALU.mult)),
                     reads=[b_l2], writes=[b_sq])
                P.op("DVE", (lambda h: h.reduce_sum(cols[:, 1:2], sqscr[:, :], axis=AXX)),
                     reads=[b_sq], writes=[b_cols])
                P.op("DVE", (lambda h: h.tensor_scalar(cols[:, 1:2], cols[:, 1:2], 1.0 / D, EPS, ALU.mult, ALU.add)),
                     reads=[b_cols], writes=[b_cols])
                P.op("ACT", (lambda h: h.activation(cols[:, 2:3], cols[:, 1:2], AF.Ln)),
                     reads=[b_cols], writes=[b_cols])
                P.op("ACT", (lambda h: h.activation(cols[:, 3:4], cols[:, 2:3], AF.Exp, scale=-0.5)),
                     reads=[b_cols], writes=[b_cols])
                P.op("DVE", (lambda h, l2=l2: h.tensor_scalar(l2[:, :], l2[:, :], cols[:, 3:4], None, ALU.mult)),
                     reads=[b_l2, b_cols], writes=[b_l2])
                P.op("GP", (lambda h, l2=l2, t=t: h.dma_start(
                    out=ln2d[t * 128:(t + 1) * 128, :], in_=l2[:, :])),
                     reads=[b_l2], writes=[b_ln2d[t % 2]], dma=True, q=f"l2{t % 2}")
                for c in range(8):
                    pt, bpt = pT[t % 2], b_pT[t % 2]
                    P.op("PE", (lambda h, c=c, pt=pt, l2=l2: h.transpose(pt[:, (c % 4) * 128:(c % 4 + 1) * 128],
                                                                         l2[:, c * 128:(c + 1) * 128], ident[:, :])),
                         reads=[b_l2, b_const], writes=[bpt])
                    P.op("DVE", (lambda h, c=c, t=t, pt=pt: h.tensor_copy(
                        ln2fm[c][:, t * 128:(t + 1) * 128], pt[:, (c % 4) * 128:(c % 4 + 1) * 128])),
                         reads=[bpt], writes=[b_ln2fm[c]])

            for ts in range(4):
                for hc in range(8):
                    P.op("PE", (lambda h, ts=ts, hc=hc: h.matmul(
                        pL[:, :], gwt[hc][:, :], ln2fm[hc][:, ts * 512:(ts + 1) * 512],
                        start=(hc == 0), stop=(hc == 7))),
                         reads=[b_const, b_ln2fm[hc]], writes=[b_pL], inc=(hc == 7))
                P.op("DVE", (lambda h, ts=ts: h.tensor_copy(lgsb[:, ts * 512:(ts + 1) * 512], pL[:, :])),
                     reads=[b_pL], writes=[b_lgsb])
            for t in range(16):
                P.op("PE", (lambda h, t=t: h.transpose(pX[:, :], lgsb[0:8, t * 128:(t + 1) * 128],
                                                       ident[0:8, 0:8])),
                     reads=[b_lgsb, b_const], writes=[b_pX])
                P.op("DVE", (lambda h, t=t: h.tensor_copy(ltokall[:, t * 8:(t + 1) * 8], pX[:, :])),
                     reads=[b_pX], writes=[b_ltokall])
            P.op("DVE", (lambda h: h.tensor_tensor(pm64[:, :], ltokall[:, 0:128:2], ltokall[:, 1:128:2], ALU.max)),
                 reads=[b_ltokall], writes=[b_pm64])
            P.op("DVE", (lambda h: h.tensor_tensor(q32a[:, :], pm64[:, 0:64:2], pm64[:, 1:64:2], ALU.max)),
                 reads=[b_pm64], writes=[b_q32a])
            P.op("DVE", (lambda h: h.tensor_tensor(w16[:, 0:16], q32a[:, 0:32:2], q32a[:, 1:32:2], ALU.max)),
                 reads=[b_q32a], writes=[b_w16])
            for t in range(16):
                P.op("DVE", (lambda h, t=t: h.tensor_scalar(
                    ltokall[:, t * 8:(t + 1) * 8], ltokall[:, t * 8:(t + 1) * 8],
                    w16[:, t:t + 1], None, ALU.subtract)),
                     reads=[b_ltokall, b_w16], writes=[b_ltokall])
            P.op("ACT", (lambda h: h.activation(gexp[:, :], ltokall[:, :], AF.Exp)),
                 reads=[b_ltokall], writes=[b_gexp])
            P.op("DVE", (lambda h: h.tensor_tensor(pm64[:, :], gexp[:, 0:128:2], gexp[:, 1:128:2], ALU.max)),
                 reads=[b_gexp], writes=[b_pm64])
            P.op("DVE", (lambda h: h.tensor_tensor(q32a[:, :], pm64[:, 0:64:2], pm64[:, 1:64:2], ALU.max)),
                 reads=[b_pm64], writes=[b_q32a])
            P.op("DVE", (lambda h: h.tensor_tensor(q32b[:, :], pm64[:, 0:64:2], pm64[:, 1:64:2], ALU.min)),
                 reads=[b_pm64], writes=[b_q32b])
            P.op("DVE", (lambda h: h.tensor_tensor(w16[:, 16:32], q32a[:, 0:32:2], q32a[:, 1:32:2], ALU.min)),
                 reads=[b_q32a], writes=[b_w16])
            P.op("DVE", (lambda h: h.tensor_tensor(w16[:, 32:48], q32b[:, 0:32:2], q32b[:, 1:32:2], ALU.max)),
                 reads=[b_q32b], writes=[b_w16])
            P.op("DVE", (lambda h: h.tensor_tensor(w16[:, 0:16], w16[:, 16:32], w16[:, 32:48], ALU.max)),
                 reads=[b_w16], writes=[b_w16])
            for t in range(16):
                P.op("DVE", (lambda h, t=t: h.tensor_scalar(
                    selm[:, t * 4:(t + 1) * 4], pm64[:, t * 4:(t + 1) * 4],
                    w16[:, t:t + 1], None, ALU.is_ge)),
                     reads=[b_pm64, b_w16], writes=[b_selm])
            P.op("DVE", (lambda h: h.tensor_tensor(ctokall[:, 0:128:2], gexp[:, 0:128:2], selm[:, 0:64], ALU.mult)),
                 reads=[b_gexp, b_selm], writes=[b_ctokall])
            P.op("DVE", (lambda h: h.tensor_tensor(ctokall[:, 1:128:2], gexp[:, 1:128:2], selm[:, 0:64], ALU.mult)),
                 reads=[b_gexp, b_selm], writes=[b_ctokall])
            P.op("DVE", (lambda h: h.tensor_tensor(pm64[:, :], ctokall[:, 0:128:2], ctokall[:, 1:128:2], ALU.max)),
                 reads=[b_ctokall], writes=[b_pm64])
            P.op("DVE", (lambda h: h.tensor_tensor(q32a[:, :], pm64[:, 0:64:2], pm64[:, 1:64:2], ALU.max)),
                 reads=[b_pm64], writes=[b_q32a])
            P.op("DVE", (lambda h: h.tensor_tensor(w16[:, 0:16], q32a[:, 0:32:2], q32a[:, 1:32:2], ALU.max)),
                 reads=[b_q32a], writes=[b_w16])
            for t in range(16):
                P.op("DVE", (lambda h, t=t: h.tensor_scalar(
                    selm[:, t * 8:(t + 1) * 8], ctokall[:, t * 8:(t + 1) * 8],
                    w16[:, t:t + 1], None, ALU.is_ge)),
                     reads=[b_ctokall, b_w16], writes=[b_selm])
            P.op("DVE", (lambda h: h.scalar_tensor_tensor(gexp[:, :], selm[:, :], -1e30, ctokall[:, :],
                                                          ALU.mult, ALU.add)),
                 reads=[b_selm, b_ctokall], writes=[b_gexp])
            P.op("DVE", (lambda h: h.tensor_tensor(pm64[:, :], gexp[:, 0:128:2], gexp[:, 1:128:2], ALU.max)),
                 reads=[b_gexp], writes=[b_pm64])
            P.op("DVE", (lambda h: h.tensor_tensor(q32a[:, :], pm64[:, 0:64:2], pm64[:, 1:64:2], ALU.max)),
                 reads=[b_pm64], writes=[b_q32a])
            P.op("DVE", (lambda h: h.tensor_tensor(w16[:, 16:32], q32a[:, 0:32:2], q32a[:, 1:32:2], ALU.max)),
                 reads=[b_q32a], writes=[b_w16])
            for t in range(16):
                P.op("DVE", (lambda h, t=t: h.tensor_scalar(
                    selm[:, t * 8:(t + 1) * 8], ctokall[:, t * 8:(t + 1) * 8],
                    w16[:, t + 16:t + 17], None, ALU.is_ge)),
                     reads=[b_ctokall, b_w16], writes=[b_selm])
            P.op("DVE", (lambda h: h.tensor_tensor(ctokall[:, :], ctokall[:, :], selm[:, :], ALU.mult)),
                 reads=[b_ctokall, b_selm], writes=[b_ctokall])
            P.op("DVE", (lambda h: h.tensor_tensor(w16[:, 32:48], w16[:, 0:16], w16[:, 16:32], ALU.add)),
                 reads=[b_w16], writes=[b_w16])
            P.op("DVE", (lambda h: h.reciprocal(w16[:, 48:64], w16[:, 32:48])),
                 reads=[b_w16], writes=[b_w16])
            for t in range(16):
                P.op("DVE", (lambda h, t=t: h.tensor_scalar(
                    ctokall[:, t * 8:(t + 1) * 8], ctokall[:, t * 8:(t + 1) * 8],
                    w16[:, t + 48:t + 49], None, ALU.mult)),
                     reads=[b_ctokall, b_w16], writes=[b_ctokall])
            P.op("DVE", (lambda h: h.tensor_scalar(selm[:, :], ctokall[:, :], 0.0, None, ALU.is_gt)),
                 reads=[b_ctokall], writes=[b_selm])
            P.op("DVE", (lambda h: h.memset(ones128[:, :], 1.0)), writes=[b_ones128])
            for tj in range(16):
                px, bpx = (pX, b_pX) if tj % 2 == 0 else (pX3, b_pX3)
                for si in range(tj + 1):
                    stat = ones128 if si < tj else tri
                    bstat = b_ones128 if si < tj else b_const
                    P.op("PE", (lambda h, si=si, tj=tj, px=px, stat=stat: h.matmul(
                        px[:, :], stat[:, :], selm[:, si * 8:(si + 1) * 8],
                        start=(si == 0), stop=(si == tj))),
                         reads=[bstat, b_selm], writes=[bpx], inc=(si == tj))
                P.op("DVE", (lambda h, tj=tj, px=px: h.tensor_copy(
                    rtokall[:, tj * 8:(tj + 1) * 8], px[:, :])),
                     reads=[bpx], writes=[b_rtokall])
            for t in range(16):
                P.op("DVE", (lambda h, t=t: h.tensor_tensor(
                    rtokall[:, t * 8:(t + 1) * 8], rtokall[:, t * 8:(t + 1) * 8], ecapm1[:, :], ALU.add)),
                     reads=[b_rtokall, b_const], writes=[b_rtokall])
            P.op("DVE", (lambda h: h.tensor_tensor(rtokall[:, :], rtokall[:, :], selm[:, :], ALU.mult)),
                 reads=[b_rtokall, b_selm], writes=[b_rtokall])
            P.op("DVE", (lambda h: h.tensor_tensor(pm64[:, :], rtokall[:, 0:128:2], rtokall[:, 1:128:2], ALU.add)),
                 reads=[b_rtokall], writes=[b_pm64])
            P.op("DVE", (lambda h: h.tensor_tensor(q32a[:, :], pm64[:, 0:64:2], pm64[:, 1:64:2], ALU.add)),
                 reads=[b_pm64], writes=[b_q32a])
            P.op("DVE", (lambda h: h.tensor_tensor(w16[:, 0:16], q32a[:, 0:32:2], q32a[:, 1:32:2], ALU.add)),
                 reads=[b_q32a], writes=[b_w16])
            P.op("DVE", (lambda h: h.tensor_scalar(gexp[:, :], selm[:, :], -1e9, 1e9, ALU.mult, ALU.add)),
                 reads=[b_selm], writes=[b_gexp])
            P.op("DVE", (lambda h: h.scalar_tensor_tensor(gexp[:, :], rtokall[:, :], 1.0, gexp[:, :],
                                                          ALU.mult, ALU.add)),
                 reads=[b_rtokall, b_gexp], writes=[b_gexp])
            P.op("DVE", (lambda h: h.tensor_tensor(pm64[:, :], gexp[:, 0:128:2], gexp[:, 1:128:2], ALU.min)),
                 reads=[b_gexp], writes=[b_pm64])
            P.op("DVE", (lambda h: h.tensor_tensor(q32a[:, :], pm64[:, 0:64:2], pm64[:, 1:64:2], ALU.min)),
                 reads=[b_pm64], writes=[b_q32a])
            P.op("DVE", (lambda h: h.tensor_tensor(w16[:, 16:32], q32a[:, 0:32:2], q32a[:, 1:32:2], ALU.min)),
                 reads=[b_q32a], writes=[b_w16])
            P.op("DVE", (lambda h: h.tensor_copy(idx2all[:, 0:32:2], w16[:, 16:32])),
                 reads=[b_w16], writes=[b_idx2all])
            P.op("DVE", (lambda h: h.tensor_tensor(idx2all[:, 1:32:2], w16[:, 0:16], w16[:, 16:32], ALU.subtract)),
                 reads=[b_w16], writes=[b_idx2all])
            for t in range(16):
                ag, bag = aug[t % 2], b_aug[t % 2]
                P.op("SP", (lambda h, ag=ag, t=t: h.dma_start(
                    out=ag[:, 0:1024], in_=ln2d[t * 128:(t + 1) * 128, :])),
                     reads=b_ln2d, writes=[bag], dma=True, q=f"ag{t % 2}")
                P.op("DVE", (lambda h, ag=ag, t=t: h.tensor_copy(
                    ag[:, 1024:1032], ctokall[:, t * 8:(t + 1) * 8])),
                     reads=[b_ctokall], writes=[bag])
                for j in range(2):
                    P.op("GP", (lambda h, ag=ag, t=t, j=j: h.indirect_dma_start(
                        out=xgd[:, :],
                        out_offset=bass.IndirectOffsetOnAxis(idx2all[:, 2 * t + j:2 * t + j + 1], 0),
                        in_=ag[:, :], in_offset=None)),
                         reads=[bag, b_idx2all, b_xgd[0]], writes=[b_xgd[1 + t % 2]],
                         dma=True, q=f"sc{t % 2}")
            P.emit(nc, sems)

        if upto == "C":
            return nc
        esL.close()

        BF = mybir.dt.bfloat16
        with contextlib.ExitStack() as esD:
            xg = [esD.enter_context(nc.sbuf_tensor(f"xg{i}", [128, 1032], F32)) for i in range(1)]
            xhi = [[esD.enter_context(nc.sbuf_tensor(f"xhi{p}_{c}", [128, CAP], BF)) for c in range(8)]
                   for p in range(2)]
            xlo = [[esD.enter_context(nc.sbuf_tensor(f"xlo{p}_{c}", [128, CAP], BF)) for c in range(8)]
                   for p in range(2)]
            wrow = [esD.enter_context(nc.sbuf_tensor(f"wrow{p}", [1, CAP], F32)) for p in range(2)]
            wbc = [esD.enter_context(nc.sbuf_tensor(f"wbc{p}", [128, CAP], F32)) for p in range(2)]
            w13s = [esD.enter_context(nc.sbuf_tensor(f"w13s{i}", [128, 1024], F32)) for i in range(2)]
            w13hi = [esD.enter_context(nc.sbuf_tensor(f"w13hi{i}", [128, 1024], BF)) for i in range(4)]
            w13lo = [esD.enter_context(nc.sbuf_tensor(f"w13lo{i}", [128, 1024], BF)) for i in range(4)]
            w2f = [esD.enter_context(nc.sbuf_tensor(f"w2f{i}", [128, 1024], F32)) for i in range(3)]
            w2hi = [esD.enter_context(nc.sbuf_tensor(f"w2hi{i}", [128, 1024], BF)) for i in range(16)]
            w2lo = [esD.enter_context(nc.sbuf_tensor(f"w2lo{i}", [128, 1024], BF)) for i in range(16)]
            acts = esD.enter_context(nc.sbuf_tensor("acts", [128, CAP], F32))
            ahi = [esD.enter_context(nc.sbuf_tensor(f"ahi{i}", [128, CAP], BF)) for i in range(16)]
            alo = [esD.enter_context(nc.sbuf_tensor(f"alo{i}", [128, CAP], BF)) for i in range(16)]
            sg = esD.enter_context(nc.sbuf_tensor("sg", [128, CAP], F32))
            ygst = [esD.enter_context(nc.sbuf_tensor(f"ygst{i}", [128, 1024], F32)) for i in range(1)]
            pG = esD.enter_context(nc.psum_tensor("pG", [128, CAP], F32))
            pU = esD.enter_context(nc.psum_tensor("pU", [128, CAP], F32))
            pY = esD.enter_context(nc.psum_tensor("pY", [128, 1024], F32))
            pE_ = esD.enter_context(nc.psum_tensor("pE", [128, 512], F32))
            pE2 = esD.enter_context(nc.psum_tensor("pE2", [128, 512], F32))

            b_xg = [Buf()]
            b_xhi = [[Buf() for _ in range(8)] for _ in range(2)]
            b_xlo = [[Buf() for _ in range(8)] for _ in range(2)]
            b_wrow = [Buf(), Buf()]
            b_wbc = [Buf(), Buf()]
            b_w13s = [Buf() for _ in range(2)]
            b_w13hi = [Buf() for _ in range(4)]
            b_w13lo = [Buf() for _ in range(4)]
            b_w2f = [Buf() for _ in range(3)]
            b_w2hi = [Buf() for _ in range(16)]
            b_w2lo = [Buf() for _ in range(16)]
            b_acts = Buf()
            b_ahi = [Buf() for _ in range(16)]
            b_alo = [Buf() for _ in range(16)]
            b_sg = Buf()
            b_ygst = [Buf()]
            b_pG, b_pU, b_pY, b_pE = Buf(), Buf(), Buf(), Buf()
            b_pE2 = Buf()

            NST = CAP // 128
            SPLITS = [(0, 512), (512, CAP)] if CAP > 512 else [(0, CAP)]

            def split(src_ap, b_src, hi_ap, b_hi, lo_ap, b_lo, w):
                P.op("ACT", (lambda h: h.copy(hi_ap, src_ap)), reads=[b_src], writes=[b_hi])
                P.op("DVE", (lambda h: h.tensor_tensor(lo_ap, src_ap, hi_ap, ALU.subtract)),
                     reads=[b_src, b_hi], writes=[b_lo])

            P.begin("D")

            def prologue_st(e, st):
                par = e % 2
                g, bg = xg[0], b_xg[0]
                P.op("GP", (lambda h, e=e, st=st, g=g: h.dma_start(
                    out=g[:, :], in_=xgd[e * CAP + st * 128:e * CAP + (st + 1) * 128, :])),
                     reads=b_xgd, writes=[bg], dma=True, q="gx0")
                for cg in range(2):
                    pe_t = pE_ if cg == 0 else pE2
                    b_pe = b_pE if cg == 0 else b_pE2
                    for c4 in range(4):
                        c = cg * 4 + c4
                        P.op("PE", (lambda h, c=c, c4=c4, g=g, pe_t=pe_t: h.transpose(
                            pe_t[:, c4 * 128:(c4 + 1) * 128],
                            g[:, c * 128:(c + 1) * 128], ident[:, :])),
                             reads=[bg, b_const], writes=[b_pe], inc=(c4 == 3))
                    for c4 in range(4):
                        c = cg * 4 + c4
                        P.op("ACT", (lambda h, c=c, c4=c4, st=st, par=par, pe_t=pe_t: h.copy(
                            xhi[par][c][:, st * 128:(st + 1) * 128],
                            pe_t[:, c4 * 128:(c4 + 1) * 128])),
                             reads=[b_pe], writes=[b_xhi[par][c]], inc=(c4 == 3))
                    for c4 in range(4):
                        c = cg * 4 + c4
                        P.op("DVE", (lambda h, c=c, c4=c4, st=st, par=par, pe_t=pe_t: h.tensor_tensor(
                            xlo[par][c][:, st * 128:(st + 1) * 128],
                            pe_t[:, c4 * 128:(c4 + 1) * 128],
                            xhi[par][c][:, st * 128:(st + 1) * 128], ALU.subtract)),
                             reads=[b_pe, b_xhi[par][c]], writes=[b_xlo[par][c]])
                P.op("PE", (lambda h, e=e, g=g: h.transpose(
                    pE_[0:1, 0:128], g[:, 1024 + e:1025 + e], ident[:, :])),
                     reads=[bg, b_const], writes=[b_pE])
                P.op("DVE", (lambda h, st=st, par=par: h.tensor_copy(
                    wrow[par][0:1, st * 128:(st + 1) * 128], pE_[0:1, 0:128])),
                     reads=[b_pE], writes=[b_wrow[par]])

            def prologue_finish(e):
                par = e % 2
                P.op("PE", (lambda h, par=par: h.matmul(
                    pE_[:, 0:512], ones1[:, :], wrow[par][0:1, 0:512], start=True, stop=True)),
                     reads=[b_wrow[par], b_const], writes=[b_pE])
                P.op("DVE", (lambda h, par=par: h.tensor_copy(wbc[par][:, 0:512], pE_[:, 0:512])),
                     reads=[b_pE], writes=[b_wbc[par]])
                P.op("PE", (lambda h, par=par: h.matmul(
                    pE_[:, 0:CAP - 512], ones1[:, :], wrow[par][0:1, 512:CAP], start=True, stop=True)),
                     reads=[b_wrow[par], b_const], writes=[b_pE])
                P.op("DVE", (lambda h, par=par: h.tensor_copy(wbc[par][:, 512:CAP], pE_[:, 0:CAP - 512])),
                     reads=[b_pE], writes=[b_wbc[par]])

            def w13_fetch(e, g16, which):
                ich = g16 + 16 * which
                wslot = (g16 % 2) * 2 + which
                ws, bws = w13s[which], b_w13s[which]
                whi, bwhi = w13hi[wslot], b_w13hi[wslot]
                wlo, bwlo = w13lo[wslot], b_w13lo[wslot]
                P.op("SP", (lambda h, e=e, ich=ich, ws=ws: h.dma_start(
                    out=ws[:, :], in_=w13b[e, ich])),
                     writes=[bws], dma=True, q=f"wa{which}")
                split(ws[:, :], bws, whi[:, :], bwhi, wlo[:, :], bwlo, 1024)

            def w2_fetch(e, ic):
                wf, bwf = w2f[ic % 3], b_w2f[ic % 3]
                P.op("SP", (lambda h, e=e, ic=ic, wf=wf: h.dma_start(
                    out=wf[:, :], in_=w2b[e, ic])),
                     writes=[bwf], dma=True, q=f"wb{ic % 3}")

            def w2_split(ic):
                wf, bwf = w2f[ic % 3], b_w2f[ic % 3]
                split(wf[:, :], bwf, w2hi[ic][:, :], b_w2hi[ic], w2lo[ic][:, :], b_w2lo[ic], 1024)

            def w2_fetch_split(e, ic):
                w2_fetch(e, ic)
                w2_split(ic)

            for st in range(NST):
                prologue_st(0, st)
            prologue_finish(0)
            for ic in range(16):
                w2_fetch_split(0, ic)
            w13_fetch(0, 0, 0)
            w13_fetch(0, 0, 1)
            for e in range(NE):
                par = e % 2
                for g16 in range(16):
                    if g16 < 15:
                        w13_fetch(e, g16 + 1, 0)
                        w13_fetch(e, g16 + 1, 1)
                    if e + 1 < NE and g16 < NST:
                        prologue_st(e + 1, g16)
                    if e + 1 < NE and g16 == NST:
                        prologue_finish(e + 1)
                    for which, ich, psum, bp in ((0, g16, pG, b_pG), (1, g16 + 16, pU, b_pU)):
                        wslot = (g16 % 2) * 2 + which
                        whi, bwhi = w13hi[wslot], b_w13hi[wslot]
                        wlo, bwlo = w13lo[wslot], b_w13lo[wslot]
                        for hc in range(8):
                            for si, (lo, hi) in enumerate(SPLITS):
                                for pi, (wop, bwop, xop, bxop) in enumerate((
                                        (whi, bwhi, xhi[par][hc], b_xhi[par][hc]),
                                        (whi, bwhi, xlo[par][hc], b_xlo[par][hc]),
                                        (wlo, bwlo, xhi[par][hc], b_xhi[par][hc]))):
                                    P.op("PE", (lambda h, wop=wop, hc=hc, lo=lo, hi=hi, psum=psum,
                                                xop=xop, pi=pi: h.matmul(
                                        psum[:, lo:hi], wop[:, hc * 128:(hc + 1) * 128],
                                        xop[:, lo:hi],
                                        start=(hc == 0 and pi == 0), stop=(hc == 7 and pi == 2),
                                        skip_group_check=True)),
                                         reads=[bwop, bxop], writes=[bp],
                                         inc=(hc == 7 and si == len(SPLITS) - 1 and pi == 2))
                    P.op("ACT", (lambda h: h.activation(sg[:, :], pG[:, :], AF.Silu)),
                         reads=[b_pG], writes=[b_sg])
                    P.op("DVE", (lambda h: h.tensor_tensor(acts[:, :], sg[:, :], pU[:, :], ALU.mult)),
                         reads=[b_sg, b_pU], writes=[b_acts])
                    P.op("DVE", (lambda h, par=par: h.tensor_tensor(acts[:, :], acts[:, :], wbc[par][:, :], ALU.mult)),
                         reads=[b_acts, b_wbc[par]], writes=[b_acts])
                    split(acts[:, :], b_acts, ahi[g16][:, :], b_ahi[g16], alo[g16][:, :], b_alo[g16], CAP)
                if e + 1 < NE:
                    w13_fetch(e + 1, 0, 0)
                    w13_fetch(e + 1, 0, 1)
                    for ic4 in range(3):
                        w2_fetch(e + 1, ic4)
                for st in range(NST):
                    for ic in range(16):
                        for h5 in range(2):
                            for pi, (aop, baop, wop, bwop) in enumerate((
                                    (ahi[ic], b_ahi[ic], w2hi[ic], b_w2hi[ic]),
                                    (ahi[ic], b_ahi[ic], w2lo[ic], b_w2lo[ic]),
                                    (alo[ic], b_alo[ic], w2hi[ic], b_w2hi[ic]))):
                                P.op("PE", (lambda h, aop=aop, wop=wop, ic=ic, st=st, h5=h5, pi=pi: h.matmul(
                                    pY[:, h5 * 512:(h5 + 1) * 512],
                                    aop[:, st * 128:(st + 1) * 128],
                                    wop[:, h5 * 512:(h5 + 1) * 512],
                                    start=(ic == 0 and pi == 0), stop=(ic == 15 and pi == 2),
                                    skip_group_check=True)),
                                     reads=[baop, bwop], writes=[b_pY],
                                     inc=((ic == 15 or st == NST - 1) and h5 == 1 and pi == 2))
                        if st == NST - 1 and e + 1 < NE:
                            w2_split(ic)
                            if ic + 3 < 16:
                                w2_fetch(e + 1, ic + 3)
                    ys, bys = ygst[0], b_ygst[0]
                    P.op("DVE", (lambda h, ys=ys: h.tensor_copy(ys[:, :], pY[:, :])),
                         reads=[b_pY], writes=[bys])
                    P.op("GP", (lambda h, e=e, st=st, ys=ys: h.dma_start(
                        out=ygd[e * CAP + st * 128:e * CAP + (st + 1) * 128, :], in_=ys[:, :])),
                         reads=[bys], writes=[b_ygd[0]], dma=True, q="yg0")
            P.emit(nc, sems)

        if upto == "D":
            return nc

        with contextlib.ExitStack() as esE:
            x1t = [esE.enter_context(nc.sbuf_tensor(f"ex1{i}", [128, D], F32)) for i in range(2)]
            g0 = [esE.enter_context(nc.sbuf_tensor(f"eg0{i}", [128, D], F32)) for i in range(2)]
            g1 = [esE.enter_context(nc.sbuf_tensor(f"eg1{i}", [128, D], F32)) for i in range(2)]
            b_x1t = [Buf(), Buf()]
            b_g0 = [Buf(), Buf()]
            b_g1 = [Buf(), Buf()]
            b_y = [Buf(), Buf()]

            P.begin("E")
            for t in range(16):
                xt_, bxt = x1t[t % 2], b_x1t[t % 2]
                ga, bga = g0[t % 2], b_g0[t % 2]
                gb, bgb = g1[t % 2], b_g1[t % 2]
                P.op("SP", (lambda h, xt_=xt_, t=t: h.dma_start(
                    out=xt_[:, :], in_=x1d[t * 128:(t + 1) * 128, :])),
                     reads=b_x1d, writes=[bxt], dma=True, q=f"ex{t % 2}")
                P.op("GP", (lambda h, ga=ga, t=t: h.indirect_dma_start(
                    out=ga[:, :], out_offset=None,
                    in_=ygd[:, :],
                    in_offset=bass.IndirectOffsetOnAxis(idx2all[:, 2 * t:2 * t + 1], 0))),
                     reads=b_ygd + [b_idx2all], writes=[bga], dma=True, q=f"e0{t % 2}")
                P.op("GP", (lambda h, gb=gb, t=t: h.indirect_dma_start(
                    out=gb[:, :], out_offset=None,
                    in_=ygd[:, :],
                    in_offset=bass.IndirectOffsetOnAxis(idx2all[:, 2 * t + 1:2 * t + 2], 0))),
                     reads=b_ygd + [b_idx2all], writes=[bgb], dma=True, q=f"e1{t % 2}")
                P.op("DVE", (lambda h, xt_=xt_, ga=ga: h.tensor_tensor(xt_[:, :], xt_[:, :], ga[:, :], ALU.add)),
                     reads=[bxt, bga], writes=[bxt])
                P.op("DVE", (lambda h, xt_=xt_, gb=gb: h.tensor_tensor(xt_[:, :], xt_[:, :], gb[:, :], ALU.add)),
                     reads=[bxt, bgb], writes=[bxt])
                P.op("GP", (lambda h, xt_=xt_, t=t: h.dma_start(
                    out=y[t * 128:(t + 1) * 128, :], in_=xt_[:, :])),
                     reads=[bxt], writes=[b_y[t % 2]], dma=True, q=f"y{t % 2}")
            for bb in b_y:
                P.seq["GP"].append(("w", bb.w[0], bb.w[1]))
            P.emit(nc, sems)

    return nc


def host_inputs(L0_attn_content, L0_attn_Wq, L0_attn_Wk, L0_attn_Wv,
                L0_ffn_gate_w, L0_ffn_w13_weight, L0_ffn_w2_weight):
    x = np.ascontiguousarray(np.asarray(L0_attn_content, dtype=np.float32))
    wq = np.ascontiguousarray(np.asarray(L0_attn_Wq, dtype=np.float32))
    wk = np.ascontiguousarray(np.asarray(L0_attn_Wk, dtype=np.float32))
    wv = np.ascontiguousarray(np.asarray(L0_attn_Wv, dtype=np.float32))
    gw = np.asarray(L0_ffn_gate_w, dtype=np.float32)
    w13 = np.asarray(L0_ffn_w13_weight, dtype=np.float32)
    w2 = np.asarray(L0_ffn_w2_weight, dtype=np.float32)

    gwt = np.ascontiguousarray(gw.T)
    w13t = w13.transpose(0, 2, 1)
    w13blk = np.ascontiguousarray(
        w13t.reshape(NE, 8, 128, 32, 128).transpose(0, 3, 2, 1, 4).reshape(NE, 32, 128, 1024))
    w2t = w2.transpose(0, 2, 1)
    w2blk = np.ascontiguousarray(w2t.reshape(NE, 16, 128, 1024))

    ident = np.eye(128, dtype=np.float32)
    kk = np.arange(128)[:, None]
    qq = np.arange(128)[None, :]
    tri = (qq >= kk).astype(np.float32)
    onesb = np.zeros((8, 128, 16), dtype=np.float32)
    for c in range(8):
        onesb[c, 0:64, 2 * c] = 1.0
        onesb[c, 64:128, 2 * c + 1] = 1.0
    ones16 = np.ones((16, 128), dtype=np.float32)
    diag16 = np.eye(16, dtype=np.float32)
    onescol = np.zeros((128, 32), dtype=np.float32)
    onescol[:, 0] = 1.0
    ones1 = np.ones((1, 128), dtype=np.float32)
    ecapm1 = np.broadcast_to((np.arange(8, dtype=np.float32) * CAP - 1.0)[None, :],
                             (128, 8)).copy()
    dumb8 = (np.arange(128, dtype=np.float32)[:, None]
             + 128.0 * np.arange(8, dtype=np.float32)[None, :]).copy()

    in_maps = []
    for core in range(8):
        b, hh = core // 2, core % 2
        if hh == 0:
            xb = np.ascontiguousarray(np.concatenate([x[b, 2048:], x[b, :2048]], axis=0))
            fl = np.zeros(N, dtype=np.float32)
            fl[2048:] = 1.0
        else:
            xb = np.ascontiguousarray(x[b])
            fl = np.ones(N, dtype=np.float32)
        flagv = np.ascontiguousarray(fl.reshape(32, 128).T)
        in_maps.append({
            "xb": xb, "wq": wq, "wk": wk, "wv": wv, "flagv": flagv, "gwt": gwt,
            "w13b": w13blk, "w2b": w2blk, "ident": ident, "tri": tri,
            "onesb": onesb, "ones16": ones16, "diag16": diag16,
            "onescol": onescol, "ones1": ones1, "ecapm1": ecapm1, "dumb8": dumb8,
        })
    return in_maps


def kernel(L0_attn_content, L0_attn_Wq, L0_attn_Wk, L0_attn_Wv,
           L0_ffn_gate_w, L0_ffn_w13_weight, L0_ffn_w2_weight):
    in_maps = host_inputs(L0_attn_content, L0_attn_Wq, L0_attn_Wk, L0_attn_Wv,
                          L0_ffn_gate_w, L0_ffn_w13_weight, L0_ffn_w2_weight)
    nc = build_nc()
    res = run_bass_kernel_spmd(nc, in_maps, core_ids=list(range(8)))
    out = np.empty((4, N, D), dtype=np.float32)
    for core in range(8):
        b, hh = core // 2, core % 2
        out[b, hh * NOWN:(hh + 1) * NOWN] = res.results[core]["y"]
    return out
